# version 06
# baseline (speedup 1.0000x reference)
import os as _os
import numpy as np
import ml_dtypes
from contextlib import ExitStack

import concourse.bass as bass
import concourse.mybir as mybir
from concourse.bass_utils import run_bass_kernel_spmd

F32 = mybir.dt.float32
BF16 = mybir.dt.bfloat16
AF = mybir.ActivationFunctionType
ALU = mybir.AluOpType
AX = mybir.AxisListType

NB = 2
S = 2048
D = 1024
C = D // 128
NH = 16
HD = 2048
NJ = HD // 128
NE = 8
NTT = S // 128
EPS = 1e-5


class Sem:
    def __init__(self, nc, ctx, name):
        self.h = ctx.enter_context(nc.semaphore(name))
        self.name = name
        self.n = 0
        self.maxwait = 0

    def inc(self, inst, k=1):
        inst.then_inc(self.h, k)
        self.n += k

    def wait(self, eng, v):
        if v <= 0:
            return
        self.maxwait = max(self.maxwait, v)
        eng.wait_ge(self.h, v)


def build_nc(dbg=(), upto="D"):
    _order = ["A", "B1", "B2", "C", "D"]
    _lim = _order.index(upto)
    def _on(ph):
        return _order.index(ph) <= _lim
    nc = bass.Bass("TRN2", target_bir_lowering=False)
    lp = nc.allow_low_precision(reason="bf16 matmul operands / fp32 accumulation per problem tolerance")
    lp.__enter__()
    ctx = ExitStack()

    x_d = nc.dram_tensor("x", [NB, S, D], F32, kind="ExternalInput").ap()
    wqT_d = nc.dram_tensor("wqT", [D, D], F32, kind="ExternalInput").ap()
    wkT_d = nc.dram_tensor("wkT", [D, D], F32, kind="ExternalInput").ap()
    wvT_d = nc.dram_tensor("wvT", [D, D], F32, kind="ExternalInput").ap()
    woT_d = nc.dram_tensor("woT", [D, D], F32, kind="ExternalInput").ap()
    gwT_d = nc.dram_tensor("gwT", [D, NE], F32, kind="ExternalInput").ap()
    gb_d = nc.dram_tensor("gb", [NE, 1], F32, kind="ExternalInput").ap()
    w1_d = nc.dram_tensor("w1", [NE, D, HD], F32, kind="ExternalInput").ap()
    b1r_d = nc.dram_tensor("b1r", [NE, 128, NJ], F32, kind="ExternalInput").ap()
    w2_d = nc.dram_tensor("w2", [NE, HD, D], F32, kind="ExternalInput").ap()
    b2_d = nc.dram_tensor("b2", [NE, D], F32, kind="ExternalInput").ap()
    identbf_d = nc.dram_tensor("identbf", [128, 128], BF16, kind="ExternalInput").ap()
    identf_d = nc.dram_tensor("identf", [NE, NE], F32, kind="ExternalInput").ap()
    out_d = nc.dram_tensor("out", [NB, S, D], F32, kind="ExternalOutput").ap()
    scr_w1_d = nc.dram_tensor("scr_w1", [NE, D, HD], BF16, kind="Internal").ap()
    scr_wo_d = nc.dram_tensor("scr_wo", [D, D], BF16, kind="Internal").ap()

    dbg_d = {}
    if "xnT" in dbg:
        dbg_d["xnT"] = nc.dram_tensor("dbg_xnT", [NB, 128, C, S], BF16, kind="ExternalOutput").ap()
        dbg_d["xntok"] = nc.dram_tensor("dbg_xntok", [NB, 128, NTT, D], BF16, kind="ExternalOutput").ap()
    if "qkv" in dbg:
        dbg_d["qT"] = nc.dram_tensor("dbg_qT", [NB, 128, C, S], BF16, kind="ExternalOutput").ap()
        dbg_d["kT"] = nc.dram_tensor("dbg_kT", [NB, 128, C, S], BF16, kind="ExternalOutput").ap()
        dbg_d["v"] = nc.dram_tensor("dbg_v", [NB, 128, NTT, NH * 65], BF16, kind="ExternalOutput").ap()
        dbg_d["wtok"] = nc.dram_tensor("dbg_wtok", [NB, 128, NTT, NE], BF16, kind="ExternalOutput").ap()
    if "ahT" in dbg:
        dbg_d["ahT"] = nc.dram_tensor("dbg_ahT", [NB, 128, C, S], BF16, kind="ExternalOutput").ap()

    names = ("bar ldC ldG mz mzV stCopy "
             "aX aV aS aT aTe aR aZ "
             "bG bGe bLt bLte bDg1 bDg2 bDg3 bWt bWte bQ bQe gQ gWv bV bVe "
             "cS cE cA cR cV cDr1 cDrb cM cDst "
             "gW1 gW2 dF dFe dG dF2 dAcc dWbD gCast stA dbgS").split()
    SN = {n: Sem(nc, ctx, n) for n in names}
    (bar, ldC, ldG, mz, mzV, stCopy,
     aX, aV, aS, aT, aTe, aR, aZ,
     bG, bGe, bLt, bLte, bDg1, bDg2, bDg3, bWt, bWte, bQ, bQe, gQ, gWv, bV, bVe,
     cS, cE, cA, cR, cV, cDr1, cDrb, cM, cDst,
     gW1, gW2, dF, dFe, dG, dF2, dAcc, dWbD, gCast, stA, dbgS) = (SN[n] for n in names)

    barK = [0]

    G = ExitStack()
    identbf = G.enter_context(nc.sbuf_tensor("identbf_s", [128, 128], BF16))
    identf = G.enter_context(nc.sbuf_tensor("identf_s", [NE, NE], F32))
    gwT = G.enter_context(nc.sbuf_tensor("gwT_s", [128, C, NE], BF16))
    gb = G.enter_context(nc.sbuf_tensor("gb_s", [1, NE], F32))
    gbb = G.enter_context(nc.sbuf_tensor("gbb_s", [128, NE], F32))
    b1 = G.enter_context(nc.sbuf_tensor("b1_s", [128, NE, NJ], F32))
    b2 = G.enter_context(nc.sbuf_tensor("b2_s", [128, D], BF16))
    wT = G.enter_context(nc.sbuf_tensor("wT_s", [128, S], BF16))
    dumV = G.enter_context(nc.sbuf_tensor("dumV", [1, 4], F32))
    dumS = G.enter_context(nc.sbuf_tensor("dumS", [1, 4], F32))

    def bar_dma(eng, sems):
        for sem in sems:
            sem.wait(eng, sem.n)
        inst = eng.nop()
        inst.then_inc(bar.h, 1)
        bar.n += 1

    def bar_tensor(t):
        t.nop().then_inc(bar.h, 1)
        bar.n += 1

    def bar_scalar(sc):
        sc.activation(dumS[:], dumS[:], AF.Copy).then_inc(bar.h, 1)
        bar.n += 1

    def bar_vector(v):
        v.tensor_copy(dumV[:], dumV[:]).then_inc(bar.h, 1)
        bar.n += 1

    def bar_wait(eng):
        bar.wait(eng, 5 * barK[0])

    def close_barrier():
        barK[0] += 1
        assert bar.n == 5 * barK[0], (bar.n, barK[0])

    for sem in SN.values():
        nc.gpsimd.sem_clear(sem.h)
    nc.all_engine_barrier()

    with nc.Block("startup") as blk:
        @blk.sync
        def _(s):
            ldC.inc(s.dma_start(out=identbf[:], in_=identbf_d[:]), 16)
            ldC.inc(s.dma_start(out=identf[:], in_=identf_d[:]), 16)
            ldC.inc(s.dma_start(out=gb[:], in_=gb_d.rearrange("e x -> x e")), 16)
            ldC.wait(s, 32)
            ldC.inc(s.dma_start(out=gbb[:], in_=gb[:].unsqueeze(1).to_broadcast([1, 128, NE])), 16)
            ldC.inc(s.dma_start(out=b1[:], in_=b1r_d.rearrange("e p j -> p e j")), 16)
            for b in range(NB):
                stCopy.inc(s.dma_start(out=out_d[b], in_=x_d[b]), 16)
            bar_dma(s, [ldC])

        @blk.vector
        def _(v):
            v.memset(wT[:], 0.0)
            mz.inc(v.memset(b2[:], 0.0), 1)
            bar_vector(v)

        @blk.gpsimd
        def _(g):
            mz.wait(g, 1)
            ldG.inc(g.dma_start(out=b2[0:NE, :], in_=b2_d[:]), 16)
            ldG.inc(g.dma_start(out=gwT[:], in_=gwT_d.rearrange("(c p) e -> p c e", p=128)), 16)
            for e in range(2):
                gCast.inc(g.dma_start(out=scr_w1_d[e], in_=w1_d[e]), 16)
            bar_dma(g, [ldG])

        @blk.tensor
        def _(t):
            bar_tensor(t)

        @blk.scalar
        def _(sc):
            bar_scalar(sc)
    close_barrier()

    for b in range(NB):
        seqS = ExitStack()
        xnT = seqS.enter_context(nc.sbuf_tensor(f"xnT{b}", [128, C, S], BF16))
        ahT = seqS.enter_context(nc.sbuf_tensor(f"ahT{b}", [128, C, 128 if _os.environ.get('KMK_TESTAH') else S], BF16))

        bcS = ExitStack()
        E = bcS.enter_context
        qT = E(nc.sbuf_tensor(f"qT{b}", [128, C, S], BF16))
        kT = E(nc.sbuf_tensor(f"kT{b}", [128, C, S], BF16))
        v_sb = E(nc.sbuf_tensor(f"v{b}", [128, NTT, NH * 65], BF16))

        with ExitStack() as abS:
            E = abS.enter_context
            xtile = E(nc.sbuf_tensor(f"xtile{b}", [128, 3, D], F32))
            xsq = E(nc.sbuf_tensor(f"xsq{b}", [128, D], F32))
            xn_tok = E(nc.sbuf_tensor(f"xn_tok{b}", [128, 3, D], BF16))
            S1 = E(nc.sbuf_tensor(f"S1{b}", [128, 128], F32))
            S2 = E(nc.sbuf_tensor(f"S2{b}", [128, 128], F32))
            negmean = E(nc.sbuf_tensor(f"negmean{b}", [128, 128], F32))
            iv = E(nc.sbuf_tensor(f"iv{b}", [128, 128], F32))
            rstd = E(nc.sbuf_tensor(f"rstd{b}", [128, 128], F32))
            ltok = E(nc.sbuf_tensor(f"ltok{b}", [128, NTT, NE], F32))
            rmax = E(nc.sbuf_tensor(f"rmax{b}", [128, NTT], F32))
            lsub = S1
            exps = negmean
            cnt_v = iv
            rsum = E(nc.sbuf_tensor(f"rsum{b}", [128, NTT], F32))
            rs = E(nc.sbuf_tensor(f"rs{b}", [128, NTT], F32))
            gtb = E(nc.sbuf_tensor(f"gtb{b}", [128, NTT * NE * NE], BF16))
            wtok = E(nc.sbuf_tensor(f"wtok{b}", [128, NTT, NE], BF16))
            qcolb = E(nc.sbuf_tensor(f"qcolb{b}", [128, 2, C, 128], BF16))
            wvTh = E(nc.sbuf_tensor(f"wvTh{b}", [128, C, 512 if _os.environ.get('KMK_VHALF') else 256], BF16))
            ps_tr = [E(nc.psum_tensor(f"ps_tr{b}_{i}", [128, 128], BF16)) for i in range(2)]
            ps_lt = [E(nc.psum_tensor(f"ps_lt{b}_{i}", [128, NE], F32)) for i in range(2)]
            ps2 = [E(nc.psum_tensor(f"ps2{b}_{i}", [128, 512], F32)) for i in range(2)]

            sched = []
            for tt in range(NTT):
                sched.append(("atr", tt))
                sched.append(("gate", tt))
            for n in range(16):
                sched += [("qkg", n, qs) for qs in range(4)]
                if 8 <= n < 12:
                    sched += [("wt", tt) for tt in range(4 * (n - 8), 4 * (n - 8) + 4)]
            _LN = bool(_os.environ.get("KMK_LNONLY"))
            _NG = bool(_os.environ.get("KMK_NOGATE"))
            NQV = 2 if _os.environ.get("KMK_VHALF") else 4
            VW = 512 if _os.environ.get("KMK_VHALF") else 256
            for qv in range(NQV):
                sched += [("vg", qv, tt) for tt in range(NTT)]
            if _LN:
                sched = [it for it in sched if it[0] == "atr"]
            elif _NG:
                keep = ("atr", "qkg", "vg", "gate") if _os.environ.get("KMK_KEEPGATE") else ("atr", "qkg", "vg")
                sched = [it for it in sched if it[0] in keep]
            elif _os.environ.get("KMK_NOWT"):
                sched = [it for it in sched if it[0] != "wt"]
            _NM = bool(_os.environ.get("KMK_NOMATH"))
            if _NM:
                sched = [it for it in sched if it[0] != "wt"]
            if _os.environ.get("KMK_SEQ"):
                _order2 = {"atr": 0, "gate": 1, "lt": 2, "wt": 3, "qkg": 4, "vg": 5}
                sched = sorted(sched, key=lambda it: _order2[it[0]])

            tr_slot = {}
            tr_wait = {}
            p2_slot = {}
            p2_wait = {}
            _trh = [None, None]
            _p2h = [None, None]
            _tri = [0]
            _p2i = [0]
            for item in sched:
                if item[0] == "atr":
                    tt = item[1]
                    for c in range(C):
                        k = ("a", tt, c)
                        sl = _tri[0] % 2; _tri[0] += 1
                        tr_slot[k] = sl
                        tr_wait[k] = _trh[sl]
                        _trh[sl] = (aTe, 8 * (NTT * b + tt) + c + 1)
                elif item[0] == "wt":
                    tt = item[1]
                    k = ("w", tt)
                    sl = _tri[0] % 2; _tri[0] += 1
                    tr_slot[k] = sl
                    tr_wait[k] = _trh[sl]
                    _trh[sl] = (bWte, NTT * b + tt + 1)
                elif item[0] == "qkg":
                    n_local, qs = item[1], item[2]
                    k = ("q", n_local, qs)
                    sl = _p2i[0] % 2; _p2i[0] += 1
                    p2_slot[k] = sl
                    p2_wait[k] = _p2h[sl]
                    _p2h[sl] = (bQe, 64 * b + 4 * n_local + qs + 1)
                elif item[0] == "vg":
                    qv, tt = item[1], item[2]
                    k = ("v", qv, tt)
                    sl = _p2i[0] % 2; _p2i[0] += 1
                    p2_slot[k] = sl
                    p2_wait[k] = _p2h[sl]
                    _p2h[sl] = (bVe, 16 * NQV * b + 16 * qv + tt + 1)

            with nc.Block(f"phAB{b}") as blk:
                @blk.sync
                def _(s):
                    bar_wait(s)
                    for tt in range(NTT):
                        idx = NTT * b + tt
                        aS.wait(s, idx - 2)
                        aX.inc(s.dma_start(out=xtile[:, tt % 3, :],
                                           in_=x_d[b, 128 * tt:128 * (tt + 1), :]), 16)
                    if "xnT" in dbg_d:
                        aTe.wait(s, 8 * NTT * (b + 1))
                        dbgS.inc(s.dma_start(out=dbg_d["xnT"][b], in_=xnT[:]), 16)
                    if "wtok" in dbg_d and not (_LN or _NG or _NM):
                        bDg3.wait(s, b + 1)
                        dbgS.inc(s.dma_start(out=dbg_d["wtok"][b], in_=wtok[:]), 16)
                    if False:
                        pass
                    if "qT" in dbg_d and not _LN:
                        bQe.wait(s, 64 * (b + 1))
                        bVe.wait(s, 16 * NQV * (b + 1))
                        dbgS.inc(s.dma_start(out=dbg_d["qT"][b], in_=qT[:]), 16)
                        dbgS.inc(s.dma_start(out=dbg_d["kT"][b], in_=kT[:]), 16)
                        dbgS.inc(s.dma_start(out=dbg_d["v"][b], in_=v_sb[:]), 16)
                    bar_dma(s, [aX, dbgS])

                @blk.gpsimd
                def _(g):
                    bar_wait(g)
                    if _LN:
                        bar_dma(g, [])
                        return
                    for n_local, (mat_d, i) in enumerate(
                            [(wqT_d, i) for i in range(C)] + [(wkT_d, i) for i in range(C)]):
                        col_g = 16 * b + n_local
                        bQ.wait(g, 4 * (col_g - 1))
                        src = mat_d[:, 128 * i:128 * (i + 1)].rearrange("(c p) i2 -> p c i2", p=128)
                        gQ.inc(g.dma_start(out=qcolb[:, n_local % 2, :, :], in_=src), 16)
                    for qv in range(NQV):
                        if qv >= 1:
                            bV.wait(g, 16 * NQV * b + 16 * qv)
                        src = wvT_d[:, VW * qv:VW * (qv + 1)].rearrange("(c p) i -> p c i", p=128)
                        gWv.inc(g.dma_start(out=wvTh[:], in_=src), 16)
                    if b == 0:
                        for e in range(2, NE):
                            gCast.inc(g.dma_start(out=scr_w1_d[e], in_=w1_d[e]), 16)
                        gCast.inc(g.dma_start(out=scr_wo_d[:], in_=woT_d[:]), 16)
                    bar_dma(g, [gQ, gWv])

                @blk.vector
                def _(v):
                    bar_wait(v)
                    for bb in range(8):
                        col0 = 64 * (bb % 2)
                        for tt in range(2 * bb, 2 * bb + 2):
                            idx = NTT * b + tt
                            aX.wait(v, 16 * (idx + 1))
                            xt = xtile[:, tt % 3, :]
                            v.tensor_tensor(xsq[:], xt, xt, ALU.mult)
                            v.reduce_sum(S1[:, col0 + tt % 2:col0 + tt % 2 + 1], xt, AX.X)
                            inst = v.reduce_sum(S2[:, col0 + tt % 2:col0 + tt % 2 + 1], xsq[:], AX.X)
                        aZ.inc(inst, 1)
                        aZ.wait(v, aZ.n)
                        cs = slice(col0, col0 + 64)
                        v.tensor_single_scalar(negmean[:, cs], S1[:, cs], -1.0 / D, ALU.mult)
                        v.tensor_tensor(S1[:, cs], negmean[:, cs], negmean[:, cs], ALU.mult)
                        v.tensor_single_scalar(S2[:, cs], S2[:, cs], 1.0 / D, ALU.mult)
                        v.tensor_tensor(S2[:, cs], S2[:, cs], S1[:, cs], ALU.subtract)
                        v.tensor_single_scalar(S2[:, cs], S2[:, cs], EPS, ALU.add)
                        aV.inc(v.reciprocal(iv[:, cs], S2[:, cs]), 1)
                        aR.wait(v, 8 * b + bb + 1)
                        for tt in range(2 * bb, 2 * bb + 2):
                            idx = NTT * b + tt
                            ci = col0 + tt % 2
                            aT.wait(v, 8 * idx - 16)
                            v.tensor_tensor(xsq[:], xtile[:, tt % 3, :],
                                            negmean[:, ci:ci + 1].to_broadcast([128, D]), ALU.add)
                            aS.inc(v.tensor_tensor(xn_tok[:, tt % 3, :], xsq[:],
                                                   rstd[:, ci:ci + 1].to_broadcast([128, D]),
                                                   ALU.mult), 1)
                    if _LN:
                        bar_vector(v)
                        return
                    if _NG or _NM:
                        ones_ap0 = v_sb[:].rearrange("p k (h x) -> p k h x", x=65)[:, :, :, 64]
                        mzV.inc(v.memset(ones_ap0, 1.0), 1)
                        bar_vector(v)
                        return
                    bGe.wait(v, NTT * (b + 1))
                    gb_b = gbb[:].unsqueeze(1).to_broadcast([128, NTT, NE])
                    v.tensor_tensor(ltok[:], ltok[:], gb_b, ALU.add)
                    lt3 = ltok[:]
                    aZ.inc(v.reduce_max(rmax[:].unsqueeze(2), lt3, AX.X), 1)
                    aZ.wait(v, aZ.n)
                    rmax_b = rmax[:].unsqueeze(2).to_broadcast([128, NTT, NE])
                    ls3 = lsub[:].rearrange("p (t e) -> p t e", e=NE)
                    bDg1.inc(v.tensor_tensor(ls3, lt3, rmax_b, ALU.subtract), 1)
                    bDg2.wait(v, b + 1)
                    ex3 = exps[:].rearrange("p (t e) -> p t e", e=NE)
                    aZ.inc(v.reduce_sum(rsum[:].unsqueeze(2), ex3, AX.X), 1)
                    aZ.wait(v, aZ.n)
                    aZ.inc(v.reciprocal(rs[:], rsum[:]), 1)
                    aZ.wait(v, aZ.n)
                    rs_b = rs[:].unsqueeze(2).to_broadcast([128, NTT, NE])
                    v.tensor_tensor(ex3, ex3, rs_b, ALU.mult)
                    pa = ex3.unsqueeze(2).to_broadcast([128, NTT, NE, NE])
                    pb = ex3.unsqueeze(3).to_broadcast([128, NTT, NE, NE])
                    gt4 = gtb[:].rearrange("p (t e j) -> p t e j", e=NE, j=NE)
                    v.tensor_tensor(gt4, pa, pb, ALU.is_gt)
                    aZ.inc(v.tensor_reduce(cnt_v[:].rearrange("p (t e) -> p t e", e=NE).unsqueeze(3), gt4, AX.X, ALU.add), 1)
                    aZ.wait(v, aZ.n)
                    aZ.inc(v.tensor_single_scalar(cnt_v[:], cnt_v[:], 1.5, ALU.is_le), 1)
                    aZ.wait(v, aZ.n)
                    bDg3.inc(v.tensor_tensor(wtok[:], ex3, cnt_v[:].rearrange("p (t e) -> p t e", e=NE), ALU.mult), 1)
                    ones_ap = v_sb[:].rearrange("p k (h x) -> p k h x", x=65)[:, :, :, 64]
                    mzV.inc(v.memset(ones_ap, 1.0), 1)
                    bar_vector(v)

                @blk.tensor
                def _(t):
                    bar_wait(t)
                    for item in sched:
                        if item[0] == "atr":
                            tt = item[1]
                            idx = NTT * b + tt
                            aS.wait(t, idx + 1)
                            for c in range(C):
                                k = ("a", tt, c)
                                if tr_wait[k] is not None:
                                    tr_wait[k][0].wait(t, tr_wait[k][1])
                                aT.inc(t.transpose(ps_tr[tr_slot[k]][:],
                                                   xn_tok[:, tt % 3, 128 * c:128 * (c + 1)],
                                                   identbf[:]), 1)
                        elif item[0] == "gate":
                            tt = item[1]
                            aTe.wait(t, 8 * (NTT * b + tt + 1))
                            bGe.wait(t, NTT * b + tt - 1)
                            for c in range(C):
                                inst = t.matmul(ps_lt[tt % 2][:],
                                                xnT[:, c, 128 * tt:128 * (tt + 1)],
                                                gwT[:, c, :],
                                                start=(c == 0), stop=(c == C - 1))
                            bG.inc(inst, 1)
                        elif item[0] == "wt":
                            tt = item[1]
                            k = ("w", tt)
                            bDg3.wait(t, b + 1)
                            if tr_wait[k] is not None:
                                tr_wait[k][0].wait(t, tr_wait[k][1])
                            bWt.inc(t.transpose(ps_tr[tr_slot[k]][0:NE, :], wtok[:, tt, :],
                                                identbf[:]), 1)
                        elif item[0] == "qkg":
                            n_local, qs = item[1], item[2]
                            k = ("q", n_local, qs)
                            col_g = 16 * b + n_local
                            if qs == 0:
                                gQ.wait(t, 16 * (col_g + 1))
                            aTe.wait(t, 8 * (NTT * b + 4 * (qs + 1)))
                            if p2_wait[k] is not None:
                                p2_wait[k][0].wait(t, p2_wait[k][1])
                            for c in range(C):
                                inst = t.matmul(ps2[p2_slot[k]][:],
                                                qcolb[:, n_local % 2, c, :],
                                                xnT[:, c, 512 * qs:512 * (qs + 1)],
                                                start=(c == 0), stop=(c == C - 1))
                            bQ.inc(inst, 1)
                        elif item[0] == "vg":
                            qv, tt = item[1], item[2]
                            k = ("v", qv, tt)
                            if tt == 0:
                                gWv.wait(t, 16 * (NQV * b + qv + 1))
                            aTe.wait(t, 8 * (NTT * b + tt + 1))
                            if p2_wait[k] is not None:
                                p2_wait[k][0].wait(t, p2_wait[k][1])
                            for c in range(C):
                                inst = t.matmul(ps2[p2_slot[k]][:, 0:VW],
                                                xnT[:, c, 128 * tt:128 * (tt + 1)],
                                                wvTh[:, c, :],
                                                start=(c == 0), stop=(c == C - 1))
                            bV.inc(inst, 1)
                    bar_tensor(t)

                @blk.scalar
                def _(sc):
                    bar_wait(sc)
                    for item in sched:
                        if item[0] == "atr":
                            tt = item[1]
                            idx = NTT * b + tt
                            if tt % 2 == 0:
                                bbb = tt // 2
                                aV.wait(sc, 8 * b + bbb + 1)
                                cs = slice(64 * (bbb % 2), 64 * (bbb % 2) + 64)
                                aR.inc(sc.activation(rstd[:, cs], iv[:, cs], AF.Sqrt), 1)
                            for c in range(C):
                                aT.wait(sc, 8 * idx + c + 1)
                                aTe.inc(sc.activation(xnT[:, c, 128 * tt:128 * (tt + 1)],
                                                      ps_tr[tr_slot[("a", tt, c)]][:], AF.Copy), 1)
                        elif item[0] == "gate":
                            tt = item[1]
                            bG.wait(sc, NTT * b + tt + 1)
                            bGe.inc(sc.activation(ltok[:, tt, :], ps_lt[tt % 2][:], AF.Copy), 1)
                            if tt == NTT - 1 and not _NM:
                                bDg1.wait(sc, b + 1)
                                bDg2.inc(sc.activation(exps[:], lsub[:], AF.Exp), 1)
                        elif item[0] == "wt":
                            tt = item[1]
                            bWt.wait(sc, NTT * b + tt + 1)
                            bWte.inc(sc.activation(wT[0:NE, 128 * tt:128 * (tt + 1)],
                                                   ps_tr[tr_slot[("w", tt)]][0:NE, :], AF.Copy), 1)
                        elif item[0] == "qkg":
                            n_local, qs = item[1], item[2]
                            dstT = qT if n_local < C else kT
                            i = n_local % C
                            u_idx = 64 * b + 4 * n_local + qs
                            bQ.wait(sc, u_idx + 1)
                            bQe.inc(sc.activation(dstT[:, i, 512 * qs:512 * (qs + 1)],
                                                  ps2[p2_slot[("q", n_local, qs)]][:], AF.Copy), 1)
                        elif item[0] == "vg":
                            qv, tt = item[1], item[2]
                            vdx = 16 * NQV * b + 16 * qv + tt
                            if qv == 0 and tt == 0:
                                mzV.wait(sc, b + 1)
                            bV.wait(sc, vdx + 1)
                            nhq = NH // NQV
                            dst = v_sb[:, tt, :].rearrange("p (h x) -> p h x", x=65)[:, nhq * qv:nhq * (qv + 1), 0:64]
                            src = ps2[p2_slot[("v", qv, tt)]][:, 0:VW].rearrange("p (h x) -> p h x", x=64)
                            bVe.inc(sc.activation(dst, src, AF.Copy), 1)
                    bar_scalar(sc)
            close_barrier()

        if not _on("C"):
            bcS.close(); seqS.close(); continue
        with ExitStack() as cS_scope:
            E3 = cS_scope.enter_context
            P = E3(nc.sbuf_tensor(f"P{b}", [128, 3, 2, 512], BF16))
            rr = E3(nc.sbuf_tensor(f"rr{b}", [1, 2, 2, 512], BF16))
            rb = E3(nc.sbuf_tensor(f"rb{b}", [64, 2, 2, 512], BF16))
            stage = E3(nc.sbuf_tensor(f"stage{b}", [64, 2, 512], BF16))
            av_sb = E3(nc.sbuf_tensor(f"av_sb{b}", [65, 2, 2, 512], F32))
            ps_sc = [E3(nc.psum_tensor(f"ps_sc{b}_{i}", [128, 1024], F32))
                     for i in range(3)]
            ps_av = [E3(nc.psum_tensor(f"ps_av{b}_{h}", [65, 512], F32))
                     for h in range(2)]

            with nc.Block(f"phC{b}") as blk:
                @blk.tensor
                def _(t):
                    bar_wait(t)
                    for ul in range(32):
                        u = 32 * b + ul
                        c, qs = ul // 4, ul % 4
                        for kt in range(NTT + 1):
                            if kt < NTT:
                                cE.wait(t, 16 * u + kt - 2)
                                t.matmul(ps_sc[kt % 3][:, 0:512],
                                         kT[0:64, c, 128 * kt:128 * (kt + 1)],
                                         qT[0:64, c, 512 * qs:512 * (qs + 1)],
                                         start=True, stop=True)
                                cS.inc(t.matmul(ps_sc[kt % 3][:, 512:1024],
                                                kT[64:128, c, 128 * kt:128 * (kt + 1)],
                                                qT[64:128, c, 512 * qs:512 * (qs + 1)],
                                                start=True, stop=True), 1)
                            if kt >= 1:
                                k2 = kt - 1
                                if k2 == 0:
                                    cV.wait(t, u)
                                cE.wait(t, 16 * u + kt)
                                t.matmul(ps_av[0][:],
                                         v_sb[:, k2, 65 * (2 * c):65 * (2 * c) + 65],
                                         P[:, k2 % 3, 0, :],
                                         start=(k2 == 0), stop=(k2 == NTT - 1))
                                cA.inc(t.matmul(ps_av[1][:],
                                                v_sb[:, k2, 65 * (2 * c + 1):65 * (2 * c + 1) + 65],
                                                P[:, k2 % 3, 1, :],
                                                start=(k2 == 0), stop=(k2 == NTT - 1)), 1)
                    bar_tensor(t)

                @blk.scalar
                def _(sc):
                    bar_wait(sc)
                    for ul in range(32):
                        u = 32 * b + ul
                        for kt in range(NTT):
                            cS.wait(sc, 16 * u + kt + 1)
                            cA.wait(sc, 16 * u + kt - 2)
                            cE.inc(sc.activation(P[:, kt % 3, :, :].rearrange("p a b -> p (a b)"),
                                                 ps_sc[kt % 3][:], AF.Exp, scale=0.125), 1)
                    bar_scalar(sc)

                @blk.vector
                def _(v):
                    bar_wait(v)
                    for ul in range(32):
                        u = 32 * b + ul
                        c, qs = ul // 4, ul % 4
                        cA.wait(v, 16 * (u + 1))
                        v.tensor_copy(av_sb[:, u % 2, 0, :], ps_av[0][:])
                        cV.inc(v.tensor_copy(av_sb[:, u % 2, 1, :], ps_av[1][:]), 1)
                        v.reciprocal(rr[:, u % 2, 0, :], av_sb[64:65, u % 2, 0, :])
                        cR.inc(v.reciprocal(rr[:, u % 2, 1, :], av_sb[64:65, u % 2, 1, :]), 1)
                        cDrb.wait(v, 32 * (u + 1))
                        v.tensor_tensor(ahT[0:64, c, 512 * qs:512 * (qs + 1)],
                                        av_sb[0:64, u % 2, 0, :], rb[:, u % 2, 0, :], ALU.mult)
                        cDst.wait(v, 16 * (u - 1))
                        cM.inc(v.tensor_tensor(stage[:, u % 2, :],
                                               av_sb[0:64, u % 2, 1, :], rb[:, u % 2, 1, :],
                                               ALU.mult), 1)
                    bar_vector(v)

                @blk.sync
                def _(s):
                    bar_wait(s)
                    for ul in range(32):
                        u = 32 * b + ul
                        c, qs = ul // 4, ul % 4
                        cR.wait(s, u + 1)
                        for h in range(2):
                            cDrb.inc(s.dma_start(out=rb[:, u % 2, h, :],
                                                 in_=rr[:, u % 2, h, :].unsqueeze(1)
                                                 .to_broadcast([1, 64, 512])), 16)
                        cM.wait(s, u + 1)
                        cDst.inc(s.dma_start(out=ahT[64:128, c, 512 * qs:512 * (qs + 1)],
                                             in_=stage[:, u % 2, :]), 16)
                    if "ahT" in dbg_d:
                        cM.wait(s, 32 * (b + 1))
                        cDst.wait(s, 16 * 32 * (b + 1))
                        dbgS.inc(s.dma_start(out=dbg_d["ahT"][b], in_=ahT[:]), 16)
                    bar_dma(s, [cDrb, cDst, dbgS])

                @blk.gpsimd
                def _(g):
                    bar_wait(g)
                    bar_dma(g, [])
            close_barrier()

        bcS.close()

        if not _on("D"):
            seqS.close(); continue
        with ExitStack() as dS_scope:
            E4 = dS_scope.enter_context
            g_sb = E4(nc.sbuf_tensor(f"g{b}", [128, 2, NJ, 512], BF16))
            w1b = E4(nc.sbuf_tensor(f"w1b{b}", [128, C, HD], BF16))
            w2q = E4(nc.sbuf_tensor(f"w2q{b}", [128, 2, NJ, 256], BF16))
            acc = E4(nc.sbuf_tensor(f"acc{b}", [128, 8, D], F32))
            wb = E4(nc.sbuf_tensor(f"wb{b}", [128, 2, 1024], BF16))
            woTs = E4(nc.sbuf_tensor(f"woTs{b}", [128, C, D], BF16))
            fp = [E4(nc.psum_tensor(f"fp{b}_{i}", [128, 512], F32)) for i in range(6)]
            pf = [E4(nc.psum_tensor(f"pf{b}_{i}", [128, 512], F32)) for i in range(2)]

            with nc.Block(f"phD{b}") as blk:
                @blk.gpsimd
                def _(g):
                    bar_wait(g)
                    if b == 0:
                        stCopy.wait(g, 32)
                    gCast.wait(g, 16 * 9)
                    ldG.inc(g.dma_start(out=woTs[:],
                                        in_=scr_wo_d.rearrange("(c p) i -> p c i", p=128)), 16)
                    for ts in range(2):
                        ts_g = 2 * b + ts
                        for e in range(NE):
                            e_g = 8 * ts_g + e
                            for oq in range(4):
                                q_g = 4 * e_g + oq
                                if q_g >= 2:
                                    qp = q_g - 2
                                    dF2.wait(g, 16 * (qp // 4) + 4 * ((qp % 4) + 1))
                                src = w2_d[e][:, 256 * oq:256 * (oq + 1)].rearrange(
                                    "(j p) o -> p j o", p=128)
                                gW2.inc(g.dma_start(out=w2q[:, q_g % 2, :, :], in_=src), 16)
                        dAcc.wait(g, 256 * (ts_g + 1))
                        dst = out_d[b].rearrange("(tt p) o -> p tt o", p=128)[:, 8 * ts:8 * (ts + 1), :]
                        stA.inc(g.dma_start(out=dst, in_=acc[:], accum_op=ALU.add), 16)
                    bar_dma(g, [ldG, gW2])

                @blk.sync
                def _(s):
                    bar_wait(s)
                    for ts in range(2):
                        ts_g = 2 * b + ts
                        for e in range(NE):
                            e_g = 8 * ts_g + e
                            if e_g == 16 * b:
                                gCast.wait(s, 16)
                                gW1.inc(s.dma_start(
                                    out=w1b[:],
                                    in_=scr_w1_d[e_g % NE].rearrange("(c p) j -> p c j", p=128)), 16)
                            dG.wait(s, 32 * (e_g - 1))
                            src = wT[e:e + 1, 1024 * ts:1024 * (ts + 1)].unsqueeze(1).to_broadcast([1, 128, 1024])
                            dWbD.inc(s.dma_start(out=wb[:, e % 2, :], in_=src), 16)
                            if e_g + 1 < 16 * (b + 1):
                                dF.wait(s, 32 * (e_g + 1))
                                gCast.wait(s, 16 * (((e_g + 1) % NE) + 1))
                                gW1.inc(s.dma_start(
                                    out=w1b[:],
                                    in_=scr_w1_d[(e_g + 1) % NE].rearrange("(c p) j -> p c j", p=128)), 16)
                    bar_dma(s, [gW1, dWbD])

                @blk.tensor
                def _(t):
                    bar_wait(t)
                    ldG.wait(t, 32 + 16 * (b + 1))
                    for ts in range(2):
                        ts_g = 2 * b + ts
                        for e in range(NE):
                            e_g = 8 * ts_g + e
                            gW1.wait(t, 16 * (e_g + 1))
                            for sub in range(2):
                                for j in range(NJ):
                                    fj = 32 * e_g + 16 * sub + j
                                    dFe.wait(t, fj - 1)
                                    for c in range(C):
                                        inst = t.matmul(pf[fj % 2][:],
                                                        w1b[:, c, 128 * j:128 * (j + 1)],
                                                        xnT[:, c, 1024 * ts + 512 * sub:1024 * ts + 512 * (sub + 1)],
                                                        start=(c == 0), stop=(c == C - 1))
                                    dF.inc(inst, 1)
                            for oq in range(4):
                                q_g = 4 * e_g + oq
                                gW2.wait(t, 16 * (q_g + 1))
                                for sub in range(2):
                                    dG.wait(t, 32 * e_g + 16 * (sub + 1))
                                    for tg in range(2):
                                        U = 16 * e_g + 4 * oq + 2 * sub + tg
                                        dAcc.wait(t, 2 * U - 4)
                                        st = U % 3
                                        for tti in range(2):
                                            tl = 2 * tg + tti
                                            ttok = 8 * ts + 4 * sub + tl
                                            bank = fp[2 * st + tti][:, 0:256]
                                            if e == 0:
                                                for c in range(C):
                                                    t.matmul(bank, ahT[:, c, 128 * ttok:128 * (ttok + 1)],
                                                             woTs[:, c, 256 * oq:256 * (oq + 1)],
                                                             start=(c == 0), stop=False)
                                                t.matmul(bank, wT[:, 128 * ttok:128 * (ttok + 1)],
                                                         b2[:, 256 * oq:256 * (oq + 1)],
                                                         start=False, stop=False)
                                        for j in range(NJ):
                                            for tti in range(2):
                                                tl = 2 * tg + tti
                                                bank = fp[2 * st + tti][:, 0:256]
                                                inst = t.matmul(bank,
                                                                g_sb[:, sub, j, 128 * tl:128 * (tl + 1)],
                                                                w2q[:, q_g % 2, j, :],
                                                                start=(e != 0 and j == 0),
                                                                stop=(j == NJ - 1))
                                        dF2.inc(inst, 1)
                    bar_tensor(t)

                @blk.scalar
                def _(sc):
                    bar_wait(sc)
                    for ts in range(2):
                        ts_g = 2 * b + ts
                        for e in range(NE):
                            e_g = 8 * ts_g + e
                            for sub in range(2):
                                dF2.wait(sc, 16 * e_g)
                                for j in range(NJ):
                                    fj = 32 * e_g + 16 * sub + j
                                    dF.wait(sc, fj + 1)
                                    dFe.inc(sc.activation(g_sb[:, sub, j, :], pf[fj % 2][:],
                                                          AF.Relu, bias=b1[:, e, j:j + 1]), 1)
                    bar_scalar(sc)

                @blk.vector
                def _(v):
                    bar_wait(v)
                    for ts in range(2):
                        ts_g = 2 * b + ts
                        for e in range(NE):
                            e_g = 8 * ts_g + e
                            dWbD.wait(v, 16 * (e_g + 1))
                            for sub in range(2):
                                for j in range(NJ):
                                    fj = 32 * e_g + 16 * sub + j
                                    dFe.wait(v, fj + 1)
                                    gsl = g_sb[:, sub, j, :]
                                    dG.inc(v.tensor_tensor(gsl, gsl,
                                                           wb[:, e % 2, 512 * sub:512 * (sub + 1)],
                                                           ALU.mult), 1)
                            for oq in range(4):
                                for sub in range(2):
                                    for tg in range(2):
                                        U = 16 * e_g + 4 * oq + 2 * sub + tg
                                        dF2.wait(v, U + 1)
                                        if e == 0 and oq == 0 and sub == 0 and tg == 0:
                                            stA.wait(v, 16 * ts_g)
                                        st = U % 3
                                        for tti in range(2):
                                            tl = 2 * tg + tti
                                            adst = acc[:, 4 * sub + tl, 256 * oq:256 * (oq + 1)]
                                            asrc = fp[2 * st + tti][:, 0:256]
                                            if e == 0:
                                                dAcc.inc(v.tensor_copy(adst, asrc), 1)
                                            else:
                                                dAcc.inc(v.tensor_tensor(adst, adst, asrc, ALU.add), 1)
                    bar_vector(v)
            close_barrier()

        seqS.close()

    with nc.Block("fin") as blk:
        @blk.sync
        def _(s):
            bar_wait(s)
            stA.wait(s, stA.n)
            if dbg_d:
                dbgS.wait(s, dbgS.n)

        @blk.gpsimd
        def _(g):
            bar_wait(g)

        @blk.tensor
        def _(t):
            bar_wait(t)

        @blk.scalar
        def _(sc):
            bar_wait(sc)

        @blk.vector
        def _(v):
            bar_wait(v)

    for sem in SN.values():
        assert sem.maxwait <= sem.n, (sem.name, sem.maxwait, sem.n)

    G.close()
    ctx.close()
    lp.__exit__(None, None, None)
    return nc


def _host_layouts(inputs):
    f32 = np.float32
    wqT = np.ascontiguousarray(np.asarray(inputs["L0_attn_Wq"], f32).T)
    wkT = np.ascontiguousarray(np.asarray(inputs["L0_attn_Wk"], f32).T)
    wvT = np.ascontiguousarray(np.asarray(inputs["L0_attn_Wv"], f32).T)
    woT = np.ascontiguousarray(np.asarray(inputs["L0_attn_Wo"], f32).T)
    gwT = np.ascontiguousarray(np.asarray(inputs["L0_ffn_gate_w"], f32).T)
    gb = np.ascontiguousarray(np.asarray(inputs["L0_ffn_gate_b"], f32).reshape(NE, 1))
    w1 = np.ascontiguousarray(np.asarray(inputs["L0_ffn_w1"], f32))
    b1r = np.ascontiguousarray(np.asarray(inputs["L0_ffn_b1"], f32).reshape(NE, NJ, 128).transpose(0, 2, 1))
    w2 = np.ascontiguousarray(np.asarray(inputs["L0_ffn_w2"], f32))
    b2 = np.ascontiguousarray(np.asarray(inputs["L0_ffn_b2"], f32))
    identbf = np.eye(128, dtype=ml_dtypes.bfloat16)
    identf = np.eye(NE, dtype=np.float32)
    return dict(wqT=wqT, wkT=wkT, wvT=wvT, woT=woT, gwT=gwT, gb=gb,
                w1=w1, b1r=b1r, w2=w2, b2=b2, identbf=identbf, identf=identf)


def kernel(L0_attn_x, L0_attn_Wq, L0_attn_Wk, L0_attn_Wv, L0_attn_Wo,
           L0_ffn_gate_w, L0_ffn_gate_b, L0_ffn_w1, L0_ffn_b1,
           L0_ffn_w2, L0_ffn_b2):
    inputs = dict(L0_attn_x=L0_attn_x, L0_attn_Wq=L0_attn_Wq, L0_attn_Wk=L0_attn_Wk,
                  L0_attn_Wv=L0_attn_Wv, L0_attn_Wo=L0_attn_Wo,
                  L0_ffn_gate_w=L0_ffn_gate_w, L0_ffn_gate_b=L0_ffn_gate_b,
                  L0_ffn_w1=L0_ffn_w1, L0_ffn_b1=L0_ffn_b1,
                  L0_ffn_w2=L0_ffn_w2, L0_ffn_b2=L0_ffn_b2)
    base = _host_layouts(inputs)
    x = np.ascontiguousarray(np.asarray(L0_attn_x, np.float32))
    nc = build_nc()
    in_maps = []
    for cix in range(8):
        m = dict(base)
        m["x"] = np.ascontiguousarray(x[2 * cix:2 * cix + 2])
        in_maps.append(m)
    res = run_bass_kernel_spmd(nc, in_maps, core_ids=list(range(8)))
    out = np.concatenate([np.asarray(r["out"], np.float32) for r in res.results], axis=0)
    return out


# version 07
# speedup vs baseline: 1.0286x; 1.0286x over previous
import os as _os
import numpy as np
import ml_dtypes
from contextlib import ExitStack

import concourse.bass as bass
import concourse.mybir as mybir
from concourse.bass_utils import run_bass_kernel_spmd

F32 = mybir.dt.float32
BF16 = mybir.dt.bfloat16
AF = mybir.ActivationFunctionType
ALU = mybir.AluOpType
AX = mybir.AxisListType

NB = 2
S = 2048
D = 1024
C = D // 128
NH = 16
HD = 2048
NJ = HD // 128
NE = 8
NTT = S // 128
EPS = 1e-5


class Sem:
    def __init__(self, nc, ctx, name):
        self.h = ctx.enter_context(nc.semaphore(name))
        self.name = name
        self.n = 0
        self.maxwait = 0

    def inc(self, inst, k=1):
        inst.then_inc(self.h, k)
        self.n += k

    def wait(self, eng, v):
        if v <= 0:
            return
        self.maxwait = max(self.maxwait, v)
        eng.wait_ge(self.h, v)


def build_nc(dbg=(), upto="D"):
    _order = ["A", "B1", "B2", "C", "D"]
    _lim = _order.index(upto)
    def _on(ph):
        return _order.index(ph) <= _lim
    nc = bass.Bass("TRN2", target_bir_lowering=False)
    lp = nc.allow_low_precision(reason="bf16 matmul operands / fp32 accumulation per problem tolerance")
    lp.__enter__()
    ctx = ExitStack()

    x_d = nc.dram_tensor("x", [NB, S, D], F32, kind="ExternalInput").ap()
    wqT_d = nc.dram_tensor("wqT", [D, D], F32, kind="ExternalInput").ap()
    wkT_d = nc.dram_tensor("wkT", [D, D], F32, kind="ExternalInput").ap()
    wvT_d = nc.dram_tensor("wvT", [D, D], F32, kind="ExternalInput").ap()
    woT_d = nc.dram_tensor("woT", [D, D], F32, kind="ExternalInput").ap()
    gwT_d = nc.dram_tensor("gwT", [D, NE], F32, kind="ExternalInput").ap()
    gb_d = nc.dram_tensor("gb", [NE, 1], F32, kind="ExternalInput").ap()
    w1_d = nc.dram_tensor("w1", [NE, D, HD], F32, kind="ExternalInput").ap()
    b1r_d = nc.dram_tensor("b1r", [128, NE * NJ], F32, kind="ExternalInput").ap()
    w2_d = nc.dram_tensor("w2", [NE, HD, D], F32, kind="ExternalInput").ap()
    b2_d = nc.dram_tensor("b2", [NE, D], F32, kind="ExternalInput").ap()
    identbf_d = nc.dram_tensor("identbf", [128, 128], BF16, kind="ExternalInput").ap()
    identf_d = nc.dram_tensor("identf", [NE, NE], F32, kind="ExternalInput").ap()
    out_d = nc.dram_tensor("out", [NB, S, D], F32, kind="ExternalOutput").ap()
    scr_w1_d = nc.dram_tensor("scr_w1", [NE, D, HD], BF16, kind="Internal").ap()
    scr_wo_d = nc.dram_tensor("scr_wo", [D, D], BF16, kind="Internal").ap()

    dbg_d = {}
    if "xnT" in dbg:
        dbg_d["xnT"] = nc.dram_tensor("dbg_xnT", [NB, 128, C, S], BF16, kind="ExternalOutput").ap()
        dbg_d["xntok"] = nc.dram_tensor("dbg_xntok", [NB, 128, NTT, D], BF16, kind="ExternalOutput").ap()
    if "qkv" in dbg:
        dbg_d["qT"] = nc.dram_tensor("dbg_qT", [NB, 128, C, S], BF16, kind="ExternalOutput").ap()
        dbg_d["kT"] = nc.dram_tensor("dbg_kT", [NB, 128, C, S], BF16, kind="ExternalOutput").ap()
        dbg_d["v"] = nc.dram_tensor("dbg_v", [NB, 128, NTT, NH * 65], BF16, kind="ExternalOutput").ap()
        dbg_d["wtok"] = nc.dram_tensor("dbg_wtok", [NB, 128, NTT, NE], BF16, kind="ExternalOutput").ap()
    if "ahT" in dbg:
        dbg_d["ahT"] = nc.dram_tensor("dbg_ahT", [NB, 128, C, S], BF16, kind="ExternalOutput").ap()

    names = ("bar ldC ldG mz mzV stCopy "
             "aX aV aS aT aTe aR aZ "
             "bG bGe bLt bLte bDg1 bDg2 bDg3 bWt bWte bQ bQe gQ gWv bV bVe "
             "cS cE cA cR cV cDr1 cDrb cM cDst "
             "gW1 gW2 dF dFe dG dF2 dAcc dWbD gCast stA dbgS").split()
    SN = {n: Sem(nc, ctx, n) for n in names}
    (bar, ldC, ldG, mz, mzV, stCopy,
     aX, aV, aS, aT, aTe, aR, aZ,
     bG, bGe, bLt, bLte, bDg1, bDg2, bDg3, bWt, bWte, bQ, bQe, gQ, gWv, bV, bVe,
     cS, cE, cA, cR, cV, cDr1, cDrb, cM, cDst,
     gW1, gW2, dF, dFe, dG, dF2, dAcc, dWbD, gCast, stA, dbgS) = (SN[n] for n in names)

    barK = [0]

    G = ExitStack()
    identbf = G.enter_context(nc.sbuf_tensor("identbf_s", [128, 128], BF16))
    identf = G.enter_context(nc.sbuf_tensor("identf_s", [NE, NE], F32))
    gwT = G.enter_context(nc.sbuf_tensor("gwT_s", [128, C, NE], BF16))
    gb = G.enter_context(nc.sbuf_tensor("gb_s", [1, NE], F32))
    gbb = G.enter_context(nc.sbuf_tensor("gbb_s", [128, NE], F32))
    b1 = G.enter_context(nc.sbuf_tensor("b1_s", [128, NE, NJ], F32))
    b2 = G.enter_context(nc.sbuf_tensor("b2_s", [128, D], BF16))
    wT = G.enter_context(nc.sbuf_tensor("wT_s", [128, S], BF16))
    dumV = G.enter_context(nc.sbuf_tensor("dumV", [1, 4], F32))
    dumS = G.enter_context(nc.sbuf_tensor("dumS", [1, 4], F32))

    def bar_dma(eng, sems):
        for sem in sems:
            sem.wait(eng, sem.n)
        inst = eng.nop()
        inst.then_inc(bar.h, 1)
        bar.n += 1

    def bar_tensor(t):
        t.nop().then_inc(bar.h, 1)
        bar.n += 1

    def bar_scalar(sc):
        sc.activation(dumS[:], dumS[:], AF.Copy).then_inc(bar.h, 1)
        bar.n += 1

    def bar_vector(v):
        v.tensor_copy(dumV[:], dumV[:]).then_inc(bar.h, 1)
        bar.n += 1

    def bar_wait(eng):
        bar.wait(eng, 5 * barK[0])

    def close_barrier():
        barK[0] += 1
        assert bar.n == 5 * barK[0], (bar.n, barK[0])

    for sem in SN.values():
        nc.gpsimd.sem_clear(sem.h)
    nc.all_engine_barrier()

    with nc.Block("startup") as blk:
        @blk.sync
        def _(s):
            ldC.inc(s.dma_start(out=identbf[:], in_=identbf_d[:]), 16)
            ldC.inc(s.dma_start(out=identf[:], in_=identf_d[:]), 16)
            ldC.inc(s.dma_start(out=gb[:], in_=gb_d.rearrange("e x -> x e")), 16)
            ldC.wait(s, 32)
            ldC.inc(s.dma_start(out=gbb[:], in_=gb[:].unsqueeze(1).to_broadcast([1, 128, NE])), 16)
            ldC.inc(s.dma_start(out=b1[:], in_=b1r_d.rearrange("p (e j) -> p e j", e=NE)), 16)
            bar_dma(s, [ldC])

        @blk.vector
        def _(v):
            v.memset(wT[:], 0.0)
            mz.inc(v.memset(b2[:], 0.0), 1)
            bar_vector(v)

        @blk.gpsimd
        def _(g):
            mz.wait(g, 1)
            ldG.inc(g.dma_start(out=b2[0:NE, :], in_=b2_d[:]), 16)
            ldG.inc(g.dma_start(out=gwT[:], in_=gwT_d.rearrange("(c p) e -> p c e", p=128)), 16)
            for e in range(2):
                gCast.inc(g.dma_start(out=scr_w1_d[e], in_=w1_d[e]), 16)
            bar_dma(g, [ldG])

        @blk.tensor
        def _(t):
            bar_tensor(t)

        @blk.scalar
        def _(sc):
            bar_scalar(sc)
    close_barrier()

    for b in range(NB):
        seqS = ExitStack()
        xnT = seqS.enter_context(nc.sbuf_tensor(f"xnT{b}", [128, C, S], BF16))
        ahT = seqS.enter_context(nc.sbuf_tensor(f"ahT{b}", [128, C, 128 if _os.environ.get('KMK_TESTAH') else S], BF16))

        bcS = ExitStack()
        E = bcS.enter_context
        qT = E(nc.sbuf_tensor(f"qT{b}", [128, C, S], BF16))
        kT = E(nc.sbuf_tensor(f"kT{b}", [128, C, S], BF16))
        v_sb = E(nc.sbuf_tensor(f"v{b}", [128, NTT, NH * 65], BF16))

        with ExitStack() as abS:
            E = abS.enter_context
            xtile = E(nc.sbuf_tensor(f"xtile{b}", [128, 3, D], F32))
            xsq = E(nc.sbuf_tensor(f"xsq{b}", [128, D], F32))
            xn_tok = E(nc.sbuf_tensor(f"xn_tok{b}", [128, 3, D], BF16))
            S1 = E(nc.sbuf_tensor(f"S1{b}", [128, 128], F32))
            S2 = E(nc.sbuf_tensor(f"S2{b}", [128, 128], F32))
            negmean = E(nc.sbuf_tensor(f"negmean{b}", [128, 128], F32))
            iv = E(nc.sbuf_tensor(f"iv{b}", [128, 128], F32))
            rstd = E(nc.sbuf_tensor(f"rstd{b}", [128, 128], F32))
            ltok = E(nc.sbuf_tensor(f"ltok{b}", [128, NTT, NE], F32))
            rmax = E(nc.sbuf_tensor(f"rmax{b}", [128, NTT], F32))
            lsub = S1
            exps = negmean
            cnt_v = iv
            rsum = E(nc.sbuf_tensor(f"rsum{b}", [128, NTT], F32))
            rs = E(nc.sbuf_tensor(f"rs{b}", [128, NTT], F32))
            gtb = E(nc.sbuf_tensor(f"gtb{b}", [128, NTT * NE * NE], BF16))
            wtok = E(nc.sbuf_tensor(f"wtok{b}", [128, NTT, NE], BF16))
            qcolb = E(nc.sbuf_tensor(f"qcolb{b}", [128, 2, C, 128], BF16))
            wvTh = E(nc.sbuf_tensor(f"wvTh{b}", [128, C, 512 if _os.environ.get('KMK_VHALF') else 256], BF16))
            ps_tr = [E(nc.psum_tensor(f"ps_tr{b}_{i}", [128, 128], BF16)) for i in range(2)]
            ps_lt = [E(nc.psum_tensor(f"ps_lt{b}_{i}", [128, NE], F32)) for i in range(2)]
            ps2 = [E(nc.psum_tensor(f"ps2{b}_{i}", [128, 512], F32)) for i in range(2)]

            sched = []
            for tt in range(NTT):
                sched.append(("atr", tt))
                sched.append(("gate", tt))
            for n in range(16):
                sched += [("qkg", n, qs) for qs in range(4)]
                if 8 <= n < 12:
                    sched += [("wt", tt) for tt in range(4 * (n - 8), 4 * (n - 8) + 4)]
            _LN = bool(_os.environ.get("KMK_LNONLY"))
            _NG = bool(_os.environ.get("KMK_NOGATE"))
            NQV = 2 if _os.environ.get("KMK_VHALF") else 4
            VW = 512 if _os.environ.get("KMK_VHALF") else 256
            for qv in range(NQV):
                sched += [("vg", qv, tt) for tt in range(NTT)]
            if _LN:
                sched = [it for it in sched if it[0] == "atr"]
            elif _NG:
                keep = ("atr", "qkg", "vg", "gate") if _os.environ.get("KMK_KEEPGATE") else ("atr", "qkg", "vg")
                sched = [it for it in sched if it[0] in keep]
            elif _os.environ.get("KMK_NOWT"):
                sched = [it for it in sched if it[0] != "wt"]
            _NM = bool(_os.environ.get("KMK_NOMATH"))
            if _NM:
                sched = [it for it in sched if it[0] != "wt"]
            if _os.environ.get("KMK_SEQ"):
                _order2 = {"atr": 0, "gate": 1, "lt": 2, "wt": 3, "qkg": 4, "vg": 5}
                sched = sorted(sched, key=lambda it: _order2[it[0]])

            tr_slot = {}
            tr_wait = {}
            p2_slot = {}
            p2_wait = {}
            _trh = [None, None]
            _p2h = [None, None]
            _tri = [0]
            _p2i = [0]
            for item in sched:
                if item[0] == "atr":
                    tt = item[1]
                    for c in range(C):
                        k = ("a", tt, c)
                        sl = _tri[0] % 2; _tri[0] += 1
                        tr_slot[k] = sl
                        tr_wait[k] = _trh[sl]
                        _trh[sl] = (aTe, 8 * (NTT * b + tt) + c + 1)
                elif item[0] == "wt":
                    tt = item[1]
                    k = ("w", tt)
                    sl = _tri[0] % 2; _tri[0] += 1
                    tr_slot[k] = sl
                    tr_wait[k] = _trh[sl]
                    _trh[sl] = (bWte, NTT * b + tt + 1)
                elif item[0] == "qkg":
                    n_local, qs = item[1], item[2]
                    k = ("q", n_local, qs)
                    sl = _p2i[0] % 2; _p2i[0] += 1
                    p2_slot[k] = sl
                    p2_wait[k] = _p2h[sl]
                    _p2h[sl] = (bQe, 64 * b + 4 * n_local + qs + 1)
                elif item[0] == "vg":
                    qv, tt = item[1], item[2]
                    k = ("v", qv, tt)
                    sl = _p2i[0] % 2; _p2i[0] += 1
                    p2_slot[k] = sl
                    p2_wait[k] = _p2h[sl]
                    _p2h[sl] = (bVe, 16 * NQV * b + 16 * qv + tt + 1)

            with nc.Block(f"phAB{b}") as blk:
                @blk.sync
                def _(s):
                    bar_wait(s)
                    for tt in range(NTT):
                        idx = NTT * b + tt
                        aS.wait(s, idx - 2)
                        aX.inc(s.dma_start(out=xtile[:, tt % 3, :],
                                           in_=x_d[b, 128 * tt:128 * (tt + 1), :]), 16)
                    stCopy.inc(s.dma_start(out=out_d[b], in_=x_d[b]), 16)
                    if "xnT" in dbg_d:
                        aTe.wait(s, 8 * NTT * (b + 1))
                        dbgS.inc(s.dma_start(out=dbg_d["xnT"][b], in_=xnT[:]), 16)
                    if "wtok" in dbg_d and not (_LN or _NG or _NM):
                        bDg3.wait(s, b + 1)
                        dbgS.inc(s.dma_start(out=dbg_d["wtok"][b], in_=wtok[:]), 16)
                    if False:
                        pass
                    if "qT" in dbg_d and not _LN:
                        bQe.wait(s, 64 * (b + 1))
                        bVe.wait(s, 16 * NQV * (b + 1))
                        dbgS.inc(s.dma_start(out=dbg_d["qT"][b], in_=qT[:]), 16)
                        dbgS.inc(s.dma_start(out=dbg_d["kT"][b], in_=kT[:]), 16)
                        dbgS.inc(s.dma_start(out=dbg_d["v"][b], in_=v_sb[:]), 16)
                    bar_dma(s, [aX, dbgS])

                @blk.gpsimd
                def _(g):
                    bar_wait(g)
                    if _LN:
                        bar_dma(g, [])
                        return
                    for n_local, (mat_d, i) in enumerate(
                            [(wqT_d, i) for i in range(C)] + [(wkT_d, i) for i in range(C)]):
                        col_g = 16 * b + n_local
                        bQ.wait(g, 4 * (col_g - 1))
                        src = mat_d[:, 128 * i:128 * (i + 1)].rearrange("(c p) i2 -> p c i2", p=128)
                        gQ.inc(g.dma_start(out=qcolb[:, n_local % 2, :, :], in_=src), 16)
                    for qv in range(NQV):
                        if qv >= 1:
                            bV.wait(g, 16 * NQV * b + 16 * qv)
                        src = wvT_d[:, VW * qv:VW * (qv + 1)].rearrange("(c p) i -> p c i", p=128)
                        gWv.inc(g.dma_start(out=wvTh[:], in_=src), 16)
                    if b == 0:
                        for e in range(2, NE):
                            gCast.inc(g.dma_start(out=scr_w1_d[e], in_=w1_d[e]), 16)
                        gCast.inc(g.dma_start(out=scr_wo_d[:], in_=woT_d[:]), 16)
                    bar_dma(g, [gQ, gWv])

                @blk.vector
                def _(v):
                    bar_wait(v)
                    for bb in range(8):
                        col0 = 64 * (bb % 2)
                        for tt in range(2 * bb, 2 * bb + 2):
                            idx = NTT * b + tt
                            aX.wait(v, 16 * (idx + 1))
                            xt = xtile[:, tt % 3, :]
                            v.tensor_tensor(xsq[:], xt, xt, ALU.mult)
                            v.reduce_sum(S1[:, col0 + tt % 2:col0 + tt % 2 + 1], xt, AX.X)
                            inst = v.reduce_sum(S2[:, col0 + tt % 2:col0 + tt % 2 + 1], xsq[:], AX.X)
                        aZ.inc(inst, 1)
                        aZ.wait(v, aZ.n)
                        cs = slice(col0, col0 + 64)
                        v.tensor_single_scalar(negmean[:, cs], S1[:, cs], -1.0 / D, ALU.mult)
                        v.tensor_tensor(S1[:, cs], negmean[:, cs], negmean[:, cs], ALU.mult)
                        v.tensor_single_scalar(S2[:, cs], S2[:, cs], 1.0 / D, ALU.mult)
                        v.tensor_tensor(S2[:, cs], S2[:, cs], S1[:, cs], ALU.subtract)
                        v.tensor_single_scalar(S2[:, cs], S2[:, cs], EPS, ALU.add)
                        aV.inc(v.reciprocal(iv[:, cs], S2[:, cs]), 1)
                        aR.wait(v, 8 * b + bb + 1)
                        for tt in range(2 * bb, 2 * bb + 2):
                            idx = NTT * b + tt
                            ci = col0 + tt % 2
                            aT.wait(v, 8 * idx - 16)
                            v.tensor_tensor(xsq[:], xtile[:, tt % 3, :],
                                            negmean[:, ci:ci + 1].to_broadcast([128, D]), ALU.add)
                            aS.inc(v.tensor_tensor(xn_tok[:, tt % 3, :], xsq[:],
                                                   rstd[:, ci:ci + 1].to_broadcast([128, D]),
                                                   ALU.mult), 1)
                    if _LN:
                        bar_vector(v)
                        return
                    if _NG or _NM:
                        ones_ap0 = v_sb[:].rearrange("p k (h x) -> p k h x", x=65)[:, :, :, 64]
                        mzV.inc(v.memset(ones_ap0, 1.0), 1)
                        bar_vector(v)
                        return
                    bGe.wait(v, NTT * (b + 1))
                    gb_b = gbb[:].unsqueeze(1).to_broadcast([128, NTT, NE])
                    v.tensor_tensor(ltok[:], ltok[:], gb_b, ALU.add)
                    lt3 = ltok[:]
                    aZ.inc(v.reduce_max(rmax[:].unsqueeze(2), lt3, AX.X), 1)
                    aZ.wait(v, aZ.n)
                    rmax_b = rmax[:].unsqueeze(2).to_broadcast([128, NTT, NE])
                    ls3 = lsub[:].rearrange("p (t e) -> p t e", e=NE)
                    bDg1.inc(v.tensor_tensor(ls3, lt3, rmax_b, ALU.subtract), 1)
                    bDg2.wait(v, b + 1)
                    ex3 = exps[:].rearrange("p (t e) -> p t e", e=NE)
                    aZ.inc(v.reduce_sum(rsum[:].unsqueeze(2), ex3, AX.X), 1)
                    aZ.wait(v, aZ.n)
                    aZ.inc(v.reciprocal(rs[:], rsum[:]), 1)
                    aZ.wait(v, aZ.n)
                    rs_b = rs[:].unsqueeze(2).to_broadcast([128, NTT, NE])
                    v.tensor_tensor(ex3, ex3, rs_b, ALU.mult)
                    pa = ex3.unsqueeze(2).to_broadcast([128, NTT, NE, NE])
                    pb = ex3.unsqueeze(3).to_broadcast([128, NTT, NE, NE])
                    gt4 = gtb[:].rearrange("p (t e j) -> p t e j", e=NE, j=NE)
                    v.tensor_tensor(gt4, pa, pb, ALU.is_gt)
                    aZ.inc(v.tensor_reduce(cnt_v[:].rearrange("p (t e) -> p t e", e=NE).unsqueeze(3), gt4, AX.X, ALU.add), 1)
                    aZ.wait(v, aZ.n)
                    aZ.inc(v.tensor_single_scalar(cnt_v[:], cnt_v[:], 1.5, ALU.is_le), 1)
                    aZ.wait(v, aZ.n)
                    bDg3.inc(v.tensor_tensor(wtok[:], ex3, cnt_v[:].rearrange("p (t e) -> p t e", e=NE), ALU.mult), 1)
                    ones_ap = v_sb[:].rearrange("p k (h x) -> p k h x", x=65)[:, :, :, 64]
                    mzV.inc(v.memset(ones_ap, 1.0), 1)
                    bar_vector(v)

                @blk.tensor
                def _(t):
                    bar_wait(t)
                    for item in sched:
                        if item[0] == "atr":
                            tt = item[1]
                            idx = NTT * b + tt
                            aS.wait(t, idx + 1)
                            for c in range(C):
                                k = ("a", tt, c)
                                if tr_wait[k] is not None:
                                    tr_wait[k][0].wait(t, tr_wait[k][1])
                                aT.inc(t.transpose(ps_tr[tr_slot[k]][:],
                                                   xn_tok[:, tt % 3, 128 * c:128 * (c + 1)],
                                                   identbf[:]), 1)
                        elif item[0] == "gate":
                            tt = item[1]
                            aTe.wait(t, 8 * (NTT * b + tt + 1))
                            bGe.wait(t, NTT * b + tt - 1)
                            for c in range(C):
                                inst = t.matmul(ps_lt[tt % 2][:],
                                                xnT[:, c, 128 * tt:128 * (tt + 1)],
                                                gwT[:, c, :],
                                                start=(c == 0), stop=(c == C - 1))
                            bG.inc(inst, 1)
                        elif item[0] == "wt":
                            tt = item[1]
                            k = ("w", tt)
                            bDg3.wait(t, b + 1)
                            if tr_wait[k] is not None:
                                tr_wait[k][0].wait(t, tr_wait[k][1])
                            bWt.inc(t.transpose(ps_tr[tr_slot[k]][0:NE, :], wtok[:, tt, :],
                                                identbf[:]), 1)
                        elif item[0] == "qkg":
                            n_local, qs = item[1], item[2]
                            k = ("q", n_local, qs)
                            col_g = 16 * b + n_local
                            if qs == 0:
                                gQ.wait(t, 16 * (col_g + 1))
                            aTe.wait(t, 8 * (NTT * b + 4 * (qs + 1)))
                            if p2_wait[k] is not None:
                                p2_wait[k][0].wait(t, p2_wait[k][1])
                            for c in range(C):
                                inst = t.matmul(ps2[p2_slot[k]][:],
                                                qcolb[:, n_local % 2, c, :],
                                                xnT[:, c, 512 * qs:512 * (qs + 1)],
                                                start=(c == 0), stop=(c == C - 1))
                            bQ.inc(inst, 1)
                        elif item[0] == "vg":
                            qv, tt = item[1], item[2]
                            k = ("v", qv, tt)
                            if tt == 0:
                                gWv.wait(t, 16 * (NQV * b + qv + 1))
                            aTe.wait(t, 8 * (NTT * b + tt + 1))
                            if p2_wait[k] is not None:
                                p2_wait[k][0].wait(t, p2_wait[k][1])
                            for c in range(C):
                                inst = t.matmul(ps2[p2_slot[k]][:, 0:VW],
                                                xnT[:, c, 128 * tt:128 * (tt + 1)],
                                                wvTh[:, c, :],
                                                start=(c == 0), stop=(c == C - 1))
                            bV.inc(inst, 1)
                    bar_tensor(t)

                @blk.scalar
                def _(sc):
                    bar_wait(sc)
                    for item in sched:
                        if item[0] == "atr":
                            tt = item[1]
                            idx = NTT * b + tt
                            if tt % 2 == 0:
                                bbb = tt // 2
                                aV.wait(sc, 8 * b + bbb + 1)
                                cs = slice(64 * (bbb % 2), 64 * (bbb % 2) + 64)
                                aR.inc(sc.activation(rstd[:, cs], iv[:, cs], AF.Sqrt), 1)
                            for c in range(C):
                                aT.wait(sc, 8 * idx + c + 1)
                                aTe.inc(sc.activation(xnT[:, c, 128 * tt:128 * (tt + 1)],
                                                      ps_tr[tr_slot[("a", tt, c)]][:], AF.Copy), 1)
                        elif item[0] == "gate":
                            tt = item[1]
                            bG.wait(sc, NTT * b + tt + 1)
                            bGe.inc(sc.activation(ltok[:, tt, :], ps_lt[tt % 2][:], AF.Copy), 1)
                            if tt == NTT - 1 and not _NM:
                                bDg1.wait(sc, b + 1)
                                bDg2.inc(sc.activation(exps[:], lsub[:], AF.Exp), 1)
                        elif item[0] == "wt":
                            tt = item[1]
                            bWt.wait(sc, NTT * b + tt + 1)
                            bWte.inc(sc.activation(wT[0:NE, 128 * tt:128 * (tt + 1)],
                                                   ps_tr[tr_slot[("w", tt)]][0:NE, :], AF.Copy), 1)
                        elif item[0] == "qkg":
                            n_local, qs = item[1], item[2]
                            dstT = qT if n_local < C else kT
                            i = n_local % C
                            u_idx = 64 * b + 4 * n_local + qs
                            bQ.wait(sc, u_idx + 1)
                            bQe.inc(sc.activation(dstT[:, i, 512 * qs:512 * (qs + 1)],
                                                  ps2[p2_slot[("q", n_local, qs)]][:], AF.Copy), 1)
                        elif item[0] == "vg":
                            qv, tt = item[1], item[2]
                            vdx = 16 * NQV * b + 16 * qv + tt
                            if qv == 0 and tt == 0:
                                mzV.wait(sc, b + 1)
                            bV.wait(sc, vdx + 1)
                            nhq = NH // NQV
                            dst = v_sb[:, tt, :].rearrange("p (h x) -> p h x", x=65)[:, nhq * qv:nhq * (qv + 1), 0:64]
                            src = ps2[p2_slot[("v", qv, tt)]][:, 0:VW].rearrange("p (h x) -> p h x", x=64)
                            bVe.inc(sc.activation(dst, src, AF.Copy), 1)
                    bar_scalar(sc)
            close_barrier()

        if not _on("C"):
            bcS.close(); seqS.close(); continue
        with ExitStack() as cS_scope:
            E3 = cS_scope.enter_context
            P = E3(nc.sbuf_tensor(f"P{b}", [128, 3, 2, 512], BF16))
            rr = E3(nc.sbuf_tensor(f"rr{b}", [1, 2, 2, 512], BF16))
            rb = E3(nc.sbuf_tensor(f"rb{b}", [64, 2, 2, 512], BF16))
            stage = E3(nc.sbuf_tensor(f"stage{b}", [64, 2, 512], BF16))
            av_sb = E3(nc.sbuf_tensor(f"av_sb{b}", [65, 2, 2, 512], F32))
            ps_sc = [E3(nc.psum_tensor(f"ps_sc{b}_{i}", [128, 1024], F32))
                     for i in range(3)]
            ps_av = [E3(nc.psum_tensor(f"ps_av{b}_{h}", [65, 512], F32))
                     for h in range(2)]

            with nc.Block(f"phC{b}") as blk:
                @blk.tensor
                def _(t):
                    bar_wait(t)
                    for ul in range(32):
                        u = 32 * b + ul
                        c, qs = ul // 4, ul % 4
                        for kt in range(NTT + 1):
                            if kt < NTT:
                                cE.wait(t, 16 * u + kt - 2)
                                t.matmul(ps_sc[kt % 3][:, 0:512],
                                         kT[0:64, c, 128 * kt:128 * (kt + 1)],
                                         qT[0:64, c, 512 * qs:512 * (qs + 1)],
                                         start=True, stop=True)
                                cS.inc(t.matmul(ps_sc[kt % 3][:, 512:1024],
                                                kT[64:128, c, 128 * kt:128 * (kt + 1)],
                                                qT[64:128, c, 512 * qs:512 * (qs + 1)],
                                                start=True, stop=True), 1)
                            if kt >= 1:
                                k2 = kt - 1
                                if k2 == 0:
                                    cV.wait(t, u)
                                cE.wait(t, 16 * u + kt)
                                t.matmul(ps_av[0][:],
                                         v_sb[:, k2, 65 * (2 * c):65 * (2 * c) + 65],
                                         P[:, k2 % 3, 0, :],
                                         start=(k2 == 0), stop=(k2 == NTT - 1))
                                cA.inc(t.matmul(ps_av[1][:],
                                                v_sb[:, k2, 65 * (2 * c + 1):65 * (2 * c + 1) + 65],
                                                P[:, k2 % 3, 1, :],
                                                start=(k2 == 0), stop=(k2 == NTT - 1)), 1)
                    bar_tensor(t)

                @blk.scalar
                def _(sc):
                    bar_wait(sc)
                    for ul in range(32):
                        u = 32 * b + ul
                        for kt in range(NTT):
                            cS.wait(sc, 16 * u + kt + 1)
                            cA.wait(sc, 16 * u + kt - 2)
                            cE.inc(sc.activation(P[:, kt % 3, :, :].rearrange("p a b -> p (a b)"),
                                                 ps_sc[kt % 3][:], AF.Exp, scale=0.125), 1)
                    bar_scalar(sc)

                @blk.vector
                def _(v):
                    bar_wait(v)
                    for ul in range(32):
                        u = 32 * b + ul
                        c, qs = ul // 4, ul % 4
                        cA.wait(v, 16 * (u + 1))
                        v.tensor_copy(av_sb[:, u % 2, 0, :], ps_av[0][:])
                        cV.inc(v.tensor_copy(av_sb[:, u % 2, 1, :], ps_av[1][:]), 1)
                        v.reciprocal(rr[:, u % 2, 0, :], av_sb[64:65, u % 2, 0, :])
                        cR.inc(v.reciprocal(rr[:, u % 2, 1, :], av_sb[64:65, u % 2, 1, :]), 1)
                        cDrb.wait(v, 32 * (u + 1))
                        v.tensor_tensor(ahT[0:64, c, 512 * qs:512 * (qs + 1)],
                                        av_sb[0:64, u % 2, 0, :], rb[:, u % 2, 0, :], ALU.mult)
                        cDst.wait(v, 16 * (u - 1))
                        cM.inc(v.tensor_tensor(stage[:, u % 2, :],
                                               av_sb[0:64, u % 2, 1, :], rb[:, u % 2, 1, :],
                                               ALU.mult), 1)
                    bar_vector(v)

                @blk.sync
                def _(s):
                    bar_wait(s)
                    for ul in range(32):
                        u = 32 * b + ul
                        c, qs = ul // 4, ul % 4
                        cR.wait(s, u + 1)
                        for h in range(2):
                            cDrb.inc(s.dma_start(out=rb[:, u % 2, h, :],
                                                 in_=rr[:, u % 2, h, :].unsqueeze(1)
                                                 .to_broadcast([1, 64, 512])), 16)
                        cM.wait(s, u + 1)
                        cDst.inc(s.dma_start(out=ahT[64:128, c, 512 * qs:512 * (qs + 1)],
                                             in_=stage[:, u % 2, :]), 16)
                    if "ahT" in dbg_d:
                        cM.wait(s, 32 * (b + 1))
                        cDst.wait(s, 16 * 32 * (b + 1))
                        dbgS.inc(s.dma_start(out=dbg_d["ahT"][b], in_=ahT[:]), 16)
                    bar_dma(s, [cDrb, cDst, dbgS])

                @blk.gpsimd
                def _(g):
                    bar_wait(g)
                    bar_dma(g, [])
            close_barrier()

        bcS.close()

        if not _on("D"):
            seqS.close(); continue
        with ExitStack() as dS_scope:
            E4 = dS_scope.enter_context
            g_sb = E4(nc.sbuf_tensor(f"g{b}", [128, 2, NJ, 512], BF16))
            w1b = E4(nc.sbuf_tensor(f"w1b{b}", [128, C, HD], BF16))
            w2q = E4(nc.sbuf_tensor(f"w2q{b}", [128, 2, NJ, 256], BF16))
            acc = E4(nc.sbuf_tensor(f"acc{b}", [128, 8, D], F32))
            wb = E4(nc.sbuf_tensor(f"wb{b}", [128, 2, 1024], BF16))
            woTs = E4(nc.sbuf_tensor(f"woTs{b}", [128, C, D], BF16))
            fp = [E4(nc.psum_tensor(f"fp{b}_{i}", [128, 512], F32)) for i in range(6)]
            pf = [E4(nc.psum_tensor(f"pf{b}_{i}", [128, 512], F32)) for i in range(2)]

            with nc.Block(f"phD{b}") as blk:
                @blk.gpsimd
                def _(g):
                    bar_wait(g)
                    stCopy.wait(g, 16 * (b + 1))
                    gCast.wait(g, 16 * 9)
                    ldG.inc(g.dma_start(out=woTs[:],
                                        in_=scr_wo_d.rearrange("(c p) i -> p c i", p=128)), 16)
                    for ts in range(2):
                        ts_g = 2 * b + ts
                        for e in range(NE):
                            e_g = 8 * ts_g + e
                            for oq in range(4):
                                q_g = 4 * e_g + oq
                                if q_g >= 2:
                                    qp = q_g - 2
                                    dF2.wait(g, 16 * (qp // 4) + 4 * ((qp % 4) + 1))
                                src = w2_d[e][:, 256 * oq:256 * (oq + 1)].rearrange(
                                    "(j p) o -> p j o", p=128)
                                gW2.inc(g.dma_start(out=w2q[:, q_g % 2, :, :], in_=src), 16)
                        dAcc.wait(g, 256 * (ts_g + 1))
                        dst = out_d[b].rearrange("(tt p) o -> p tt o", p=128)[:, 8 * ts:8 * (ts + 1), :]
                        stA.inc(g.dma_start(out=dst, in_=acc[:], accum_op=ALU.add), 16)
                    bar_dma(g, [ldG, gW2])

                @blk.sync
                def _(s):
                    bar_wait(s)
                    for ts in range(2):
                        ts_g = 2 * b + ts
                        for e in range(NE):
                            e_g = 8 * ts_g + e
                            if e_g == 16 * b:
                                gCast.wait(s, 16)
                                gW1.inc(s.dma_start(
                                    out=w1b[:],
                                    in_=scr_w1_d[e_g % NE].rearrange("(c p) j -> p c j", p=128)), 16)
                            dG.wait(s, 32 * (e_g - 1))
                            src = wT[e:e + 1, 1024 * ts:1024 * (ts + 1)].unsqueeze(1).to_broadcast([1, 128, 1024])
                            dWbD.inc(s.dma_start(out=wb[:, e % 2, :], in_=src), 16)
                            if e_g + 1 < 16 * (b + 1):
                                dF.wait(s, 32 * (e_g + 1))
                                gCast.wait(s, 16 * (((e_g + 1) % NE) + 1))
                                gW1.inc(s.dma_start(
                                    out=w1b[:],
                                    in_=scr_w1_d[(e_g + 1) % NE].rearrange("(c p) j -> p c j", p=128)), 16)
                    bar_dma(s, [gW1, dWbD])

                @blk.tensor
                def _(t):
                    bar_wait(t)
                    ldG.wait(t, 32 + 16 * (b + 1))
                    for ts in range(2):
                        ts_g = 2 * b + ts
                        for e in range(NE):
                            e_g = 8 * ts_g + e
                            gW1.wait(t, 16 * (e_g + 1))
                            for sub in range(2):
                                for j in range(NJ):
                                    fj = 32 * e_g + 16 * sub + j
                                    dFe.wait(t, fj - 1)
                                    for c in range(C):
                                        inst = t.matmul(pf[fj % 2][:],
                                                        w1b[:, c, 128 * j:128 * (j + 1)],
                                                        xnT[:, c, 1024 * ts + 512 * sub:1024 * ts + 512 * (sub + 1)],
                                                        start=(c == 0), stop=(c == C - 1))
                                    dF.inc(inst, 1)
                            for oq in range(4):
                                q_g = 4 * e_g + oq
                                gW2.wait(t, 16 * (q_g + 1))
                                for sub in range(2):
                                    dG.wait(t, 32 * e_g + 16 * (sub + 1))
                                    for tg in range(2):
                                        U = 16 * e_g + 4 * oq + 2 * sub + tg
                                        dAcc.wait(t, 2 * U - 4)
                                        st = U % 3
                                        for tti in range(2):
                                            tl = 2 * tg + tti
                                            ttok = 8 * ts + 4 * sub + tl
                                            bank = fp[2 * st + tti][:, 0:256]
                                            if e == 0:
                                                for c in range(C):
                                                    t.matmul(bank, ahT[:, c, 128 * ttok:128 * (ttok + 1)],
                                                             woTs[:, c, 256 * oq:256 * (oq + 1)],
                                                             start=(c == 0), stop=False)
                                                t.matmul(bank, wT[:, 128 * ttok:128 * (ttok + 1)],
                                                         b2[:, 256 * oq:256 * (oq + 1)],
                                                         start=False, stop=False)
                                        for j in range(NJ):
                                            for tti in range(2):
                                                tl = 2 * tg + tti
                                                bank = fp[2 * st + tti][:, 0:256]
                                                inst = t.matmul(bank,
                                                                g_sb[:, sub, j, 128 * tl:128 * (tl + 1)],
                                                                w2q[:, q_g % 2, j, :],
                                                                start=(e != 0 and j == 0),
                                                                stop=(j == NJ - 1))
                                        dF2.inc(inst, 1)
                    bar_tensor(t)

                @blk.scalar
                def _(sc):
                    bar_wait(sc)
                    for ts in range(2):
                        ts_g = 2 * b + ts
                        for e in range(NE):
                            e_g = 8 * ts_g + e
                            for sub in range(2):
                                dF2.wait(sc, 16 * e_g)
                                for j in range(NJ):
                                    fj = 32 * e_g + 16 * sub + j
                                    dF.wait(sc, fj + 1)
                                    dFe.inc(sc.activation(g_sb[:, sub, j, :], pf[fj % 2][:],
                                                          AF.Relu, bias=b1[:, e, j:j + 1]), 1)
                    bar_scalar(sc)

                @blk.vector
                def _(v):
                    bar_wait(v)
                    for ts in range(2):
                        ts_g = 2 * b + ts
                        for e in range(NE):
                            e_g = 8 * ts_g + e
                            dWbD.wait(v, 16 * (e_g + 1))
                            for sub in range(2):
                                for j in range(NJ):
                                    fj = 32 * e_g + 16 * sub + j
                                    dFe.wait(v, fj + 1)
                                    gsl = g_sb[:, sub, j, :]
                                    dG.inc(v.tensor_tensor(gsl, gsl,
                                                           wb[:, e % 2, 512 * sub:512 * (sub + 1)],
                                                           ALU.mult), 1)
                            for oq in range(4):
                                for sub in range(2):
                                    for tg in range(2):
                                        U = 16 * e_g + 4 * oq + 2 * sub + tg
                                        dF2.wait(v, U + 1)
                                        if e == 0 and oq == 0 and sub == 0 and tg == 0:
                                            stA.wait(v, 16 * ts_g)
                                        st = U % 3
                                        for tti in range(2):
                                            tl = 2 * tg + tti
                                            adst = acc[:, 4 * sub + tl, 256 * oq:256 * (oq + 1)]
                                            asrc = fp[2 * st + tti][:, 0:256]
                                            if e == 0:
                                                dAcc.inc(v.tensor_copy(adst, asrc), 1)
                                            else:
                                                dAcc.inc(v.tensor_tensor(adst, adst, asrc, ALU.add), 1)
                    bar_vector(v)
            close_barrier()

        seqS.close()

    with nc.Block("fin") as blk:
        @blk.sync
        def _(s):
            bar_wait(s)
            stA.wait(s, stA.n)
            if dbg_d:
                dbgS.wait(s, dbgS.n)

        @blk.gpsimd
        def _(g):
            bar_wait(g)

        @blk.tensor
        def _(t):
            bar_wait(t)

        @blk.scalar
        def _(sc):
            bar_wait(sc)

        @blk.vector
        def _(v):
            bar_wait(v)

    for sem in SN.values():
        assert sem.maxwait <= sem.n, (sem.name, sem.maxwait, sem.n)

    G.close()
    ctx.close()
    lp.__exit__(None, None, None)
    return nc


def _host_layouts(inputs):
    f32 = np.float32
    wqT = np.ascontiguousarray(np.asarray(inputs["L0_attn_Wq"], f32).T)
    wkT = np.ascontiguousarray(np.asarray(inputs["L0_attn_Wk"], f32).T)
    wvT = np.ascontiguousarray(np.asarray(inputs["L0_attn_Wv"], f32).T)
    woT = np.ascontiguousarray(np.asarray(inputs["L0_attn_Wo"], f32).T)
    gwT = np.ascontiguousarray(np.asarray(inputs["L0_ffn_gate_w"], f32).T)
    gb = np.ascontiguousarray(np.asarray(inputs["L0_ffn_gate_b"], f32).reshape(NE, 1))
    w1 = np.ascontiguousarray(np.asarray(inputs["L0_ffn_w1"], f32))
    b1r = np.ascontiguousarray(np.asarray(inputs["L0_ffn_b1"], f32).reshape(NE, NJ, 128).transpose(2, 0, 1).reshape(128, NE * NJ))
    w2 = np.ascontiguousarray(np.asarray(inputs["L0_ffn_w2"], f32))
    b2 = np.ascontiguousarray(np.asarray(inputs["L0_ffn_b2"], f32))
    identbf = np.eye(128, dtype=ml_dtypes.bfloat16)
    identf = np.eye(NE, dtype=np.float32)
    return dict(wqT=wqT, wkT=wkT, wvT=wvT, woT=woT, gwT=gwT, gb=gb,
                w1=w1, b1r=b1r, w2=w2, b2=b2, identbf=identbf, identf=identf)


def kernel(L0_attn_x, L0_attn_Wq, L0_attn_Wk, L0_attn_Wv, L0_attn_Wo,
           L0_ffn_gate_w, L0_ffn_gate_b, L0_ffn_w1, L0_ffn_b1,
           L0_ffn_w2, L0_ffn_b2):
    inputs = dict(L0_attn_x=L0_attn_x, L0_attn_Wq=L0_attn_Wq, L0_attn_Wk=L0_attn_Wk,
                  L0_attn_Wv=L0_attn_Wv, L0_attn_Wo=L0_attn_Wo,
                  L0_ffn_gate_w=L0_ffn_gate_w, L0_ffn_gate_b=L0_ffn_gate_b,
                  L0_ffn_w1=L0_ffn_w1, L0_ffn_b1=L0_ffn_b1,
                  L0_ffn_w2=L0_ffn_w2, L0_ffn_b2=L0_ffn_b2)
    base = _host_layouts(inputs)
    x = np.ascontiguousarray(np.asarray(L0_attn_x, np.float32))
    nc = build_nc()
    in_maps = []
    for cix in range(8):
        m = dict(base)
        m["x"] = np.ascontiguousarray(x[2 * cix:2 * cix + 2])
        in_maps.append(m)
    res = run_bass_kernel_spmd(nc, in_maps, core_ids=list(range(8)))
    out = np.concatenate([np.asarray(r["out"], np.float32) for r in res.results], axis=0)
    return out
